# version 03
# baseline (speedup 1.0000x reference)
import numpy as np
import ml_dtypes
import dataclasses
from contextlib import ExitStack
import concourse.bass as bass
import concourse.mybir as mybir
from concourse.alu_op_type import AluOpType as OP
from concourse.bass_utils import run_bass_kernel_spmd

F32 = mybir.dt.float32
BF16 = mybir.dt.bfloat16
AF = mybir.ActivationFunctionType
AX = mybir.AxisListType
BF = ml_dtypes.bfloat16

D = 2048
S = 4096
NB = 4
TOK = 2048
NCH = 16
NTT = 16
TB = 512
NTB = TOK // TB
NEG = -1.0e30
SCALE = float(1.0 / np.sqrt(128.0))
EPS_LN = 1e-5
EPS_RMS = 1e-6


def pos_map(parity):
    tiles = [2 * j + parity for j in range(NTT)]
    return np.concatenate([np.arange(128 * t, 128 * t + 128) for t in tiles])


def rope_tables(positions):
    inv = 1.0 / (10000.0 ** (np.arange(0, 128, 2, dtype=np.float64) / 128.0))
    ang = positions[:, None].astype(np.float64) * inv[None, :]
    c = np.cos(ang).astype(np.float32)
    s = np.sin(ang).astype(np.float32)
    return (np.ascontiguousarray(np.concatenate([c, c], 1).T),
            np.ascontiguousarray(np.concatenate([s, s], 1).T))


def causal_mask(parity):
    i = np.arange(128)[:, None]
    kk = np.arange(256)[None, :]
    return np.where(kk <= parity * 128 + i, 0.0, NEG).astype(np.float32)


def perm_matrix():
    P = np.zeros((128, 128), np.float32)
    for m in range(64):
        P[m + 64, m] = -1.0
        P[m, m + 64] = 1.0
    return P.astype(BF)


def col_layout(v):
    n = v.shape[0] // 128
    return np.ascontiguousarray(np.asarray(v, np.float32).reshape(n, 128).T)


def bcast_rows(v, rows=128):
    return np.ascontiguousarray(np.broadcast_to(np.asarray(v, np.float32), (rows, len(v))))


def esel_matrix():
    E = np.zeros((8, 8 * 128), np.float32)
    for e in range(8):
        E[e, e * 128:(e + 1) * 128] = 1.0
    return E



_REAL_OPS = frozenset((
    "matmul", "transpose", "ldweights", "activation", "tensor_tensor",
    "tensor_scalar", "tensor_tensor_reduce", "scalar_tensor_tensor",
    "tensor_copy", "tensor_reduce", "reduce_max", "reduce_sum", "reciprocal",
    "memset", "iota", "select", "copy", "add", "mul", "sqrt", "square",
    "sign", "bn_stats", "bn_aggr", "dma_start", "tensor_add", "tensor_mul",
    "tensor_sub", "tensor_max"))


class EngProxy:
    def __init__(self, eng, p, name):
        self._eng = eng
        self._p = p
        self._name = name

    def __getattr__(self, n):
        a = getattr(self._eng, n)
        if callable(a) and n in _REAL_OPS:
            if n == "dma_start":
                def call(*args, **kw):
                    r = a(*args, **kw)
                    self._eng.nop()
                    return r
            else:
                def call(*args, **kw):
                    r = a(*args, **kw)
                    self._p._last[self._name] = r
                    return r
            return call
        return a


class Prog:
    def __init__(self, name="prog"):
        self.nc = bass.Bass("TRN2", target_bir_lowering=False)
        self.ctx = ExitStack()
        self.cnt = {}
        self.semh = {}
        self.nbar = 0
        self.name = name
        self._last = {}
        self._who = {}
        self.ntag = 0

    def sem(self, name):
        if name not in self.semh:
            self.semh[name] = self.ctx.enter_context(self.nc.semaphore(name))
            self.cnt[name] = 0
        return self.semh[name]

    DMA_SEMS = ("dL", "dG", "dW0", "dW1", "dWv", "dx0", "dx1", "dxx",
                "dh0", "dh1", "dso", "dsa0", "dsa1", "dxa0", "dxa1",
                "dk0", "dk1", "dq0", "dq1", "dos", "dac", "dcn", "dcc")

    def start(self):
        for n in self.DMA_SEMS + ("pe", "ve", "se", "ge", "bar"):
            self.sem(n)
        self.blk = self.ctx.enter_context(self.nc.Block())

    def close(self):
        self.ctx.close()

    def inc(self, instr, name, amt=1):
        instr.then_inc(self.semh[name], amt)
        self.cnt[name] += amt
        self._who[name] = instr
        return self.cnt[name]

    def w(self, eng, name, val):
        if val > 0:
            eng.wait_ge(self.semh[name], val)

    def fence(self, eng, name):
        last = self._last.get(eng._name)
        if last is None:
            return
        if self._who.get(name) is last:
            v = self.cnt[name]
        else:
            v = self.inc(last, name)
        eng.wait_ge(self.semh[name], v)

    def on(self, engname, f):
        def wrapped(eng):
            f(EngProxy(eng, self, engname))
        getattr(self.blk, {"t": "tensor", "v": "vector", "a": "scalar",
                           "g": "gpsimd", "s": "sync"}[engname])(wrapped)

    def linc(self, eng, name, amt=1):
        last = self._last[eng._name]
        if self._who.get(name) is last:
            return self.cnt[name]
        return self.inc(last, name, amt)

    def barrier(self):
        self.nbar += 1
        tgt = 5 * self.nbar
        sem = self.semh["bar"]
        dvals = [(n, self.cnt[n]) for n in self.DMA_SEMS if self.cnt[n] > 0]
        own = {"t": "pe", "v": "ve", "a": "se", "g": "ge"}
        for en in ("t", "v", "a", "g", "s"):
            def f(eng, en=en):
                if eng._name in ("s", "g"):
                    for n, v in dvals:
                        self.w(eng, n, v)
                if en in own:
                    self.fence(eng, own[en])
                eng._eng.nop().then_inc(sem, 1)
                eng.wait_ge(sem, tgt)
            self.on(en, f)
        self.cnt["bar"] = tgt


def mkap(t_ap, offset, dims):
    return dataclasses.replace(t_ap, offset=t_ap.offset + offset,
                               ap=[list(d) for d in dims])



def alloc_common(p):
    c, nc = p.ctx, p.nc
    U = ""
    b = {}
    b["idsrc"] = c.enter_context(nc.sbuf_tensor("idsrc" + U, [128, 128], F32))
    b["identb"] = c.enter_context(nc.sbuf_tensor("identb" + U, [128, 128], BF16))
    b["identf"] = c.enter_context(nc.sbuf_tensor("identf" + U, [128, 128], F32))
    b["ones1"] = c.enter_context(nc.sbuf_tensor("ones1" + U, [1, 128], F32))
    b["onescol"] = c.enter_context(nc.sbuf_tensor("onescol" + U, [128, 1], BF16))
    b["cpool"] = c.enter_context(nc.sbuf_tensor("cpool" + U, [128, 8], F32))
    for i, val in enumerate([0.0, EPS_LN, EPS_RMS, -3.0e38]):
        nc.const_aps.register(F32, val, b["cpool"][:, i:i + 1])
    return b


def emit_prologue(p, b):
    def g(eng):
        p.inc(eng.iota(b["idsrc"][:], pattern=[[1, 128]], base=0, channel_multiplier=-1,
                       allow_small_or_imprecise_dtypes=True), "ge")
        p.inc(eng.memset(b["ones1"][:], 1.0), "ge")
        p.inc(eng.memset(b["onescol"][:], 1.0), "ge")
        for i, val in enumerate([0.0, EPS_LN, EPS_RMS, -3.0e38]):
            p.inc(eng.memset(b["cpool"][:, i:i + 1], val), "ge")
    p.on("g", g)

    def v(eng):
        p.w(eng, "ge", p.cnt["ge"])
        eng.tensor_scalar(out=b["identb"][:], in0=b["idsrc"][:], scalar1=0.0,
                          scalar2=None, op0=OP.is_equal)
        eng.tensor_scalar(out=b["identf"][:], in0=b["idsrc"][:], scalar1=0.0,
                          scalar2=None, op0=OP.is_equal)
        p.linc(eng, "ve")
    p.on("v", v)
    p.barrier()



def load_fm(p, eng, dst, src_ap, ncols, row0, col0, ndc, width, sem="dL",
            per=4, dst_col0=0, chunk_stride=None):
    cs = chunk_stride if chunk_stride is not None else width
    tot = p.cnt[sem]
    dc = 0
    while dc < ndc:
        k = min(per, ndc - dc)
        src = mkap(src_ap, (row0 + dc * 128) * ncols + col0,
                   [[ncols, 128], [128 * ncols, k], [1, width]])
        d = dst[:, dst_col0 + dc * cs: dst_col0 + dc * cs + (k - 1) * cs + width]
        d = mkap(d, 0, [list(d.ap[0]), [cs, k], [1, width]])
        tot = p.inc(eng.dma_start(out=d, in_=src), sem, 16)
        dc += k
    return tot


def store_fm(p, eng, src, dst_ap, ncols, row0, col0, ndc, width, sem="dL",
             per=4, src_col0=0, chunk_stride=None):
    cs = chunk_stride if chunk_stride is not None else width
    tot = p.cnt[sem]
    dc = 0
    while dc < ndc:
        k = min(per, ndc - dc)
        dst = mkap(dst_ap, (row0 + dc * 128) * ncols + col0,
                   [[ncols, 128], [128 * ncols, k], [1, width]])
        s = src[:, src_col0 + dc * cs: src_col0 + dc * cs + (k - 1) * cs + width]
        s = mkap(s, 0, [list(s.ap[0]), [cs, k], [1, width]])
        tot = p.inc(eng.dma_start(out=dst, in_=s), sem, 16)
        dc += k
    return tot



def _uniq(p):
    p.ntag += 1
    return f"_{p.ntag}"


def emit_ln_phase(p, b, src_dram, dst_dram=None, dsth_dram=None, y_dram=None,
                  ncols=TOK):
    nc = p.nc
    c = ExitStack()
    U = _uniq(p)
    xb = [c.enter_context(nc.sbuf_tensor(f"lnx{i}" + U, [128, NCH * TB], F32)) for i in range(2)]
    xh = c.enter_context(nc.sbuf_tensor("lnxh" + U, [128, NCH * TB], BF16))
    sq = c.enter_context(nc.sbuf_tensor("lnsq" + U, [128, NCH * TB], BF16))
    ob = c.enter_context(nc.sbuf_tensor("lno" + U, [128, NCH * TB], F32))
    ohb = c.enter_context(nc.sbuf_tensor("lnoh" + U, [128, NCH * TB], BF16))
    rows = c.enter_context(nc.sbuf_tensor("lnrows" + U, [1, 5 * TB], F32))
    yb = c.enter_context(nc.sbuf_tensor("lny" + U, [128, D], F32)) if y_dram is not None else None
    psA = c.enter_context(nc.psum_tensor("lnpA" + U, [1, TB], F32))
    psB = c.enter_context(nc.psum_tensor("lnpB" + U, [1, TB], F32))
    psM = c.enter_context(nc.psum_tensor("lnpM" + U, [128, TB], F32))
    psR = c.enter_context(nc.psum_tensor("lnpR" + U, [128, TB], F32))
    psT = [c.enter_context(nc.psum_tensor(f"lnpT{i}" + U, [128, 128], F32))
           for i in range(2)] if y_dram is not None else None

    war_x = [0, 0]
    war_store = 0
    for tb in range(ncols // TB):
        bi = tb % 2
        xt = xb[bi]
        res = {}

        def s(eng, tb=tb, bi=bi, xt=xt):
            p.w(eng, "ve", war_x[bi])
            res["ld"] = load_fm(p, eng, xt, src_dram, ncols, 0, tb * TB, NCH, TB,
                                sem=f"dx{bi}")
        p.on("s", s)

        def a(eng, tb=tb, xt=xt, bi=bi):
            p.w(eng, f"dx{bi}", res["ld"])
            p.w(eng, "pe", p.cnt["pe"])
            for dc in range(NCH):
                sl = slice(dc * TB, (dc + 1) * TB)
                eng.activation(out=xh[:, sl], in_=xt[:, sl], func=AF.Copy)
            p.fence(eng, "se")
            for dc in range(NCH):
                sl = slice(dc * TB, (dc + 1) * TB)
                eng.activation(out=sq[:, sl], in_=xh[:, sl], func=AF.Square)
            p.linc(eng, "se")
        p.on("a", a)
        se_cast = p.cnt["se"]

        def t(eng, tb=tb):
            p.w(eng, "se", se_cast)
            p.w(eng, "ve", p.cnt["ve"])
            for dc in range(NCH):
                sl = slice(dc * TB, (dc + 1) * TB)
                eng.matmul(psA[:], b["onescol"][:], xh[:, sl],
                           start=(dc == 0), stop=(dc == NCH - 1))
            for dc in range(NCH):
                sl = slice(dc * TB, (dc + 1) * TB)
                eng.matmul(psB[:], b["onescol"][:], sq[:, sl],
                           start=(dc == 0), stop=(dc == NCH - 1))
            p.linc(eng, "pe")
        p.on("t", t)
        pe_sums = p.cnt["pe"]

        r_m, r_q, r_v, r_s, r_r = (rows[:, i * TB:(i + 1) * TB] for i in range(5))

        def v(eng, tb=tb):
            p.w(eng, "pe", pe_sums)
            eng.tensor_scalar(out=r_m, in0=psA[:], scalar1=1.0 / D, scalar2=None, op0=OP.mult)
            eng.tensor_scalar(out=r_q, in0=psB[:], scalar1=1.0 / D, scalar2=None, op0=OP.mult)
            p.fence(eng, "ve")
            eng.scalar_tensor_tensor(out=r_v, in0=r_m, scalar=-1.0, in1=r_m,
                                     op0=OP.mult, op1=OP.mult)
            p.fence(eng, "ve")
            eng.tensor_tensor(out=r_v, in0=r_q, in1=r_v, op=OP.add)
            p.linc(eng, "ve")
        p.on("v", v)
        ve_var = p.cnt["ve"]

        def a2(eng, tb=tb):
            p.w(eng, "ve", ve_var)
            eng.activation(out=r_s, in_=r_v, func=AF.Sqrt, bias=EPS_LN)
            p.linc(eng, "se")
        p.on("a", a2)
        se_sqrt = p.cnt["se"]

        def v2(eng, tb=tb):
            p.w(eng, "se", se_sqrt)
            eng.reciprocal(r_r, r_s)
            p.fence(eng, "ve")
            p.linc(eng, "ve")
        p.on("v", v2)
        ve_rstd = p.cnt["ve"]

        def t2(eng, tb=tb):
            p.w(eng, "ve", ve_rstd)
            eng.matmul(psM[:], b["ones1"][:], r_m, start=True, stop=True)
            eng.matmul(psR[:], b["ones1"][:], r_r, start=True, stop=True)
            p.linc(eng, "pe")
        p.on("t", t2)
        pe_bc = p.cnt["pe"]

        def v3(eng, tb=tb, xt=xt):
            p.w(eng, "pe", pe_bc)
            p.w(eng, "dso", war_store)
            for dc in range(NCH):
                sl = slice(dc * TB, (dc + 1) * TB)
                eng.tensor_tensor(out=ob[:, sl], in0=xt[:, sl], in1=psM[:], op=OP.subtract)
            p.fence(eng, "ve")
            for dc in range(NCH):
                sl = slice(dc * TB, (dc + 1) * TB)
                eng.tensor_tensor(out=ob[:, sl], in0=ob[:, sl], in1=psR[:], op=OP.mult)
            p.linc(eng, "ve")
        p.on("v", v3)
        ve_apply = p.cnt["ve"]
        war_x[bi] = ve_apply

        if y_dram is None:
            def a3(eng, tb=tb):
                p.w(eng, "ve", ve_apply)
                for dc in range(NCH):
                    sl = slice(dc * TB, (dc + 1) * TB)
                    eng.activation(out=ohb[:, sl], in_=ob[:, sl], func=AF.Copy)
                p.linc(eng, "se")
            p.on("a", a3)
            se_h = p.cnt["se"]

            def s2(eng, tb=tb):
                p.w(eng, "ve", ve_apply)
                store_fm(p, eng, ob, dst_dram, ncols, 0, tb * TB, NCH, TB, sem="dso")
                p.w(eng, "se", se_h)
                res["st"] = store_fm(p, eng, ohb, dsth_dram, ncols, 0, tb * TB, NCH, TB,
                                     sem="dso")
            p.on("s", s2)
            war_store = res["st"]
        else:
            for tt in range(TB // 128):
                ve0 = p.cnt["ve"]
                pe_marks = []

                def t3(eng, tb=tb, tt=tt, ve0=ve0):
                    p.w(eng, "ve", max(ve_apply, ve0))
                    for dc in range(NCH):
                        if dc >= 2:
                            p.w(eng, "ve", ve0 + dc - 1)
                        pe_marks.append(p.inc(
                            eng.transpose(psT[dc % 2][:],
                                          ob[:, dc * TB + tt * 128: dc * TB + (tt + 1) * 128],
                                          b["identf"][:]), "pe"))
                p.on("t", t3)

                def v4(eng, tb=tb, tt=tt):
                    p.w(eng, "dso", war_store)
                    for dc in range(NCH):
                        p.w(eng, "pe", pe_marks[dc])
                        p.inc(eng.tensor_copy(yb[:, dc * 128:(dc + 1) * 128],
                                              psT[dc % 2][:]), "ve")
                p.on("v", v4)
                assert p.cnt["ve"] == ve0 + NCH
                ve_y = p.cnt["ve"]

                def s3(eng, tb=tb, tt=tt):
                    p.w(eng, "ve", ve_y)
                    row0 = tb * TB + tt * 128
                    dst = mkap(y_dram, row0 * D, [[D, 128], [1, D]])
                    res["sty"] = p.inc(eng.dma_start(out=dst, in_=yb[:]), "dso", 16)
                p.on("s", s3)
                war_store = res["sty"]
    p.barrier()
    c.close()



def emit_rms_phase(p, b, src_dram, h_dram, lnw_col_s, ncols=TOK):
    nc = p.nc
    c = ExitStack()
    U = _uniq(p)
    xb = [c.enter_context(nc.sbuf_tensor(f"rmx{i}" + U, [128, NCH * TB], F32)) for i in range(2)]
    xh = c.enter_context(nc.sbuf_tensor("rmxh" + U, [128, NCH * TB], BF16))
    sq = c.enter_context(nc.sbuf_tensor("rmsq" + U, [128, NCH * TB], BF16))
    hb = c.enter_context(nc.sbuf_tensor("rmh" + U, [128, NCH * TB], BF16))
    rows = c.enter_context(nc.sbuf_tensor("rmrows" + U, [1, 3 * TB], F32))
    psB = c.enter_context(nc.psum_tensor("rmpB" + U, [1, TB], F32))
    psR = c.enter_context(nc.psum_tensor("rmpR" + U, [128, TB], F32))

    war_x = [0, 0]
    war_store = 0
    for tb in range(ncols // TB):
        bi = tb % 2
        xt = xb[bi]
        res = {}

        def s(eng, tb=tb, bi=bi, xt=xt):
            p.w(eng, "ve", war_x[bi])
            res["ld"] = load_fm(p, eng, xt, src_dram, ncols, 0, tb * TB, NCH, TB,
                                sem=f"dx{bi}")
        p.on("s", s)

        def a(eng, tb=tb, xt=xt, bi=bi):
            p.w(eng, f"dx{bi}", res["ld"])
            p.w(eng, "pe", p.cnt["pe"])
            for dc in range(NCH):
                sl = slice(dc * TB, (dc + 1) * TB)
                eng.activation(out=xh[:, sl], in_=xt[:, sl], func=AF.Copy)
            p.fence(eng, "se")
            for dc in range(NCH):
                sl = slice(dc * TB, (dc + 1) * TB)
                eng.activation(out=sq[:, sl], in_=xh[:, sl], func=AF.Square)
            p.linc(eng, "se")
        p.on("a", a)
        se_cast = p.cnt["se"]

        def t(eng, tb=tb):
            p.w(eng, "se", se_cast)
            p.w(eng, "ve", p.cnt["ve"])
            for dc in range(NCH):
                sl = slice(dc * TB, (dc + 1) * TB)
                eng.matmul(psB[:], b["onescol"][:], sq[:, sl],
                           start=(dc == 0), stop=(dc == NCH - 1))
            p.linc(eng, "pe")
        p.on("t", t)
        pe_sums = p.cnt["pe"]

        r_q, r_s, r_r = (rows[:, i * TB:(i + 1) * TB] for i in range(3))

        def v(eng, tb=tb):
            p.w(eng, "pe", pe_sums)
            eng.tensor_scalar(out=r_q, in0=psB[:], scalar1=1.0 / D, scalar2=None, op0=OP.mult)
            p.linc(eng, "ve")
        p.on("v", v)
        ve_q = p.cnt["ve"]

        def a2(eng, tb=tb):
            p.w(eng, "ve", ve_q)
            eng.activation(out=r_s, in_=r_q, func=AF.Sqrt, bias=EPS_RMS)
            p.linc(eng, "se")
        p.on("a", a2)
        se_sqrt = p.cnt["se"]

        def v2(eng, tb=tb):
            p.w(eng, "se", se_sqrt)
            eng.reciprocal(r_r, r_s)
            p.fence(eng, "ve")
            p.linc(eng, "ve")
        p.on("v", v2)
        ve_rstd = p.cnt["ve"]

        def t2(eng, tb=tb):
            p.w(eng, "ve", ve_rstd)
            eng.matmul(psR[:], b["ones1"][:], r_r, start=True, stop=True)
            p.linc(eng, "pe")
        p.on("t", t2)
        pe_bc = p.cnt["pe"]

        def v3(eng, tb=tb, xt=xt):
            p.w(eng, "pe", pe_bc)
            p.w(eng, "dso", war_store)
            for dc in range(NCH):
                sl = slice(dc * TB, (dc + 1) * TB)
                eng.scalar_tensor_tensor(out=hb[:, sl], in0=xt[:, sl],
                                         scalar=lnw_col_s[:, dc:dc + 1], in1=psR[:],
                                         op0=OP.mult, op1=OP.mult)
            p.linc(eng, "ve")
        p.on("v", v3)
        ve_apply = p.cnt["ve"]
        war_x[bi] = ve_apply

        def s2(eng, tb=tb):
            p.w(eng, "ve", ve_apply)
            res["st"] = store_fm(p, eng, hb, h_dram, ncols, 0, tb * TB, NCH, TB,
                                 sem="dso")
        p.on("s", s2)
        war_store = res["st"]
    p.barrier()
    c.close()



def emit_ffn_h(p, b, xh_dram, wg_dram, wu_dram, h_dram):
    nc = p.nc
    c = ExitStack()
    U = _uniq(p)
    wgs = c.enter_context(nc.sbuf_tensor("f_wg" + U, [128, NCH * 1024], BF16))
    wus = c.enter_context(nc.sbuf_tensor("f_wu" + U, [128, NCH * 1024], BF16))
    xht = [c.enter_context(nc.sbuf_tensor(f"f_xh{i}" + U, [128, NCH * TB], BF16)) for i in range(2)]
    sg = [c.enter_context(nc.sbuf_tensor(f"f_sg{i}" + U, [128, TB], F32)) for i in range(2)]
    hst = c.enter_context(nc.sbuf_tensor("f_h" + U, [128, 8 * TB], BF16))
    psG = [c.enter_context(nc.psum_tensor(f"f_pG{i}" + U, [128, TB], F32)) for i in range(2)]
    psU = [c.enter_context(nc.psum_tensor(f"f_pU{i}" + U, [128, TB], F32)) for i in range(2)]

    war_x = [0, 0]
    for hb in range(8):
        res = {}

        def g(eng, hb=hb):
            p.w(eng, "pe", p.cnt["pe"])
            res["wg"] = load_fm(p, eng, wgs, wg_dram, 8192, 0, hb * 1024, NCH, 1024, sem="dG")
            res["wu"] = load_fm(p, eng, wus, wu_dram, 8192, 0, hb * 1024, NCH, 1024, sem="dG")
        p.on("g", g)
        ld_w = None

        for tb in range(NTB):
            bi = tb % 2
            xt = xht[bi]

            def s(eng, tb=tb, bi=bi, xt=xt):
                p.w(eng, "pe", war_x[bi])
                res["x"] = load_fm(p, eng, xt, xh_dram, TOK, 0, tb * TB, NCH, TB,
                                   sem=f"dx{bi}")
            p.on("s", s)
            ld_x = res["x"]

            pe_marks = []
            ve0 = p.cnt["ve"]
            se0 = p.cnt["se"]

            def t(eng, tb=tb, xt=xt, bi=bi):
                p.w(eng, f"dx{bi}", ld_x)
                p.w(eng, "dG", res["wu"])
                for hc in range(8):
                    pi = hc % 2
                    if hc >= 2:
                        p.w(eng, "se", se0 + hc - 1)
                        p.w(eng, "ve", ve0 + hc - 1)
                    for dc in range(NCH):
                        eng.matmul(psG[pi][:], wgs[:, dc * 1024 + hc * 128: dc * 1024 + (hc + 1) * 128],
                                   xt[:, dc * TB:(dc + 1) * TB],
                                   start=(dc == 0), stop=(dc == NCH - 1))
                    p.linc(eng, "pe")
                    for dc in range(NCH):
                        eng.matmul(psU[pi][:], wus[:, dc * 1024 + hc * 128: dc * 1024 + (hc + 1) * 128],
                                   xt[:, dc * TB:(dc + 1) * TB],
                                   start=(dc == 0), stop=(dc == NCH - 1))
                    pe_marks.append(p.linc(eng, "pe"))
            p.on("t", t)
            war_x[bi] = p.cnt["pe"]

            def a(eng, tb=tb):
                for hc in range(8):
                    p.w(eng, "pe", pe_marks[hc])
                    if hc >= 2:
                        p.w(eng, "ve", ve0 + hc - 1)
                    p.inc(eng.activation(out=sg[hc % 2][:], in_=psG[hc % 2][:],
                                         func=AF.Silu), "se")
            p.on("a", a)
            assert p.cnt["se"] == se0 + 8

            def v(eng, tb=tb):
                p.w(eng, "dso", p.cnt["dso"])
                for hc in range(8):
                    p.w(eng, "se", se0 + hc + 1)
                    p.inc(eng.tensor_tensor(out=hst[:, hc * TB:(hc + 1) * TB],
                                            in0=sg[hc % 2][:], in1=psU[hc % 2][:],
                                            op=OP.mult), "ve")
            p.on("v", v)
            assert p.cnt["ve"] == ve0 + 8
            ve_h = p.cnt["ve"]

            def s2(eng, tb=tb, hb=hb):
                p.w(eng, "ve", ve_h)
                store_fm(p, eng, hst, h_dram, TOK, hb * 1024, tb * TB, 8, TB, sem="dso")
            p.on("s", s2)
    p.barrier()
    c.close()


def emit_ffn_down(p, b, h_dram, wd_dram, x_dram, out_dram, hid=8192, nblk9=None,
                  bias_cols=None):
    nc = p.nc
    c = ExitStack()
    U = _uniq(p)
    nhk = hid // 128
    wds = c.enter_context(nc.sbuf_tensor("d_wd" + U, [128, nhk * TB], BF16))
    hbuf = c.enter_context(nc.sbuf_tensor("d_h" + U, [128, nhk * TB], BF16))
    xc = [c.enter_context(nc.sbuf_tensor(f"d_x{i}" + U, [128, TB], F32)) for i in range(2)]
    ob = [c.enter_context(nc.sbuf_tensor(f"d_o{i}" + U, [128, TB], F32)) for i in range(2)]
    psD = [c.enter_context(nc.psum_tensor(f"d_p{i}" + U, [128, TB], F32)) for i in range(2)]

    war_st = [0, 0]
    for dcg in range(4):
        def g(eng, dcg=dcg):
            p.w(eng, "pe", p.cnt["pe"])
            res_w = load_fm(p, eng, wds, wd_dram, D, 0, dcg * TB, nhk, TB, sem="dG", per=8)
            p._wd = res_w
        p.on("g", g)
        ld_w = p._wd

        for tb in range(NTB):
            def s(eng, tb=tb):
                p.w(eng, "pe", p.cnt["pe"])
                p._h = load_fm(p, eng, hbuf, h_dram, TOK, 0, tb * TB, nhk, TB, per=16,
                               sem="dh0")
            p.on("s", s)
            ld_h = p._h

            for ocl in range(4):
                oc = dcg * 4 + ocl
                pi = (tb * 4 + ocl) % 2
                res = {}

                def s2(eng, oc=oc, tb=tb, pi=pi):
                    p.w(eng, "ve", p.cnt["ve"])
                    src = mkap(x_dram, (oc * 128) * TOK + tb * TB,
                               [[TOK, 128], [1, TB]])
                    res["x"] = p.inc(eng.dma_start(out=xc[pi][:], in_=src),
                                     f"dxa{pi}", 16)
                p.on("s", s2)

                def t(eng, oc=oc, tb=tb, pi=pi, ocl=ocl):
                    p.w(eng, "dG", ld_w)
                    p.w(eng, "dh0", ld_h)
                    p.w(eng, "ve", p.cnt["ve"])
                    for hk in range(nhk):
                        eng.matmul(psD[pi][:], wds[:, hk * TB + ocl * 128: hk * TB + (ocl + 1) * 128],
                                   hbuf[:, hk * TB:(hk + 1) * TB],
                                   start=(hk == 0), stop=(hk == nhk - 1))
                    p.linc(eng, "pe")
                p.on("t", t)
                pe_mm = p.cnt["pe"]

                def v(eng, oc=oc, tb=tb, pi=pi):
                    p.w(eng, "pe", pe_mm)
                    p.w(eng, f"dxa{pi}", res["x"])
                    p.w(eng, f"dsa{pi}", war_st[pi])
                    eng.tensor_tensor(out=ob[pi][:], in0=xc[pi][:], in1=psD[pi][:], op=OP.add)
                    p.linc(eng, "ve")
                p.on("v", v)
                ve_o = p.cnt["ve"]

                def s3(eng, oc=oc, tb=tb, pi=pi):
                    p.w(eng, "ve", ve_o)
                    dst = mkap(out_dram, (oc * 128) * TOK + tb * TB,
                               [[TOK, 128], [1, TB]])
                    war_st[pi] = p.inc(eng.dma_start(out=dst, in_=ob[pi][:]),
                                       f"dsa{pi}", 16)
                p.on("s", s3)
    p.barrier()
    c.close()



def emit_moe_routing(p, b, xh_dram, gate_dram, ebias_s, s_rows, wall):
    nc = p.nc
    c = ExitStack()
    U = _uniq(p)
    gts = c.enter_context(nc.sbuf_tensor("r_gt" + U, [128, NCH * 8], F32))
    xht = [c.enter_context(nc.sbuf_tensor(f"r_xh{i}" + U, [128, NCH * TB], F32)) for i in range(2)]
    tmp = c.enter_context(nc.sbuf_tensor("r_tmp" + U, [128, 8 * 4], F32))
    mcol = c.enter_context(nc.sbuf_tensor("r_m" + U, [128, 2], F32))
    psW = c.enter_context(nc.psum_tensor("r_pW" + U, [128, 8], F32))
    psWT = c.enter_context(nc.psum_tensor("r_pWT" + U, [8, 128], F32))

    def g(eng):
        p._g = load_fm(p, eng, gts, gate_dram, 8, 0, 0, NCH, 8, sem="dG", per=16)
    p.on("g", g)
    ld_g = p._g

    ldx = [0, 0]
    for tt in range(NTT):
        tb = tt // 4
        bi = tb % 2
        if tt % 4 == 0:
            def s(eng, tb=tb, bi=bi):
                p.w(eng, "pe", p.cnt["pe"])
                ldx[bi] = load_fm(p, eng, xht[bi], xh_dram, TOK, 0, tb * TB, NCH, TB,
                                  sem=f"dx{bi}")
            p.on("s", s)

        def t(eng, tt=tt, tb=tb, bi=bi):
            p.w(eng, f"dx{bi}", ldx[bi])
            p.w(eng, "dG", ld_g)
            p.w(eng, "se", p.cnt["se"])
            for dc in range(NCH):
                eng.matmul(psW[:], xht[bi][:, dc * TB + (tt % 4) * 128: dc * TB + (tt % 4) * 128 + 128],
                           gts[:, dc * 8:(dc + 1) * 8],
                           start=(dc == 0), stop=(dc == NCH - 1))
            p.linc(eng, "pe")
        p.on("t", t)
        pe_mm = p.cnt["pe"]

        def a(eng, tt=tt):
            p.w(eng, "pe", pe_mm)
            p.w(eng, "ve", p.cnt["ve"])
            eng.activation(out=wall[:, tt * 8:(tt + 1) * 8], in_=psW[:], func=AF.Tanh,
                           scale=0.5)
            p.linc(eng, "se")
        p.on("a", a)
        se_sig = p.cnt["se"]

        def v(eng, tt=tt):
            p.w(eng, "se", se_sig)
            wsl = wall[:, tt * 8:(tt + 1) * 8]
            eng.tensor_scalar(out=wsl, in0=wsl, scalar1=0.5, scalar2=0.5,
                              op0=OP.mult, op1=OP.add)
            p.fence(eng, "ve")
            t0 = tmp[:, 0:8]
            t1 = tmp[:, 8:16]
            t2 = tmp[:, 16:24]
            t3 = tmp[:, 24:32]
            eng.tensor_tensor(out=t0, in0=wsl, in1=ebias_s[:], op=OP.add)
            p.fence(eng, "ve")
            eng.tensor_reduce(mcol[:, 0:1], t0, axis=AX.X, op=OP.max)
            p.fence(eng, "ve")
            eng.tensor_scalar(out=t1, in0=t0, scalar1=mcol[:, 0:1], scalar2=None, op0=OP.is_ge)
            p.fence(eng, "ve")
            eng.scalar_tensor_tensor(out=t2, in0=t1, scalar=NEG, in1=t0,
                                     op0=OP.mult, op1=OP.add)
            p.fence(eng, "ve")
            eng.tensor_reduce(mcol[:, 1:2], t2, axis=AX.X, op=OP.max)
            p.fence(eng, "ve")
            eng.tensor_scalar(out=t3, in0=t2, scalar1=mcol[:, 1:2], scalar2=None, op0=OP.is_ge)
            p.fence(eng, "ve")
            eng.tensor_tensor(out=t1, in0=t1, in1=t3, op=OP.add)
            p.fence(eng, "ve")
            eng.tensor_tensor(out=wsl, in0=t1, in1=wsl, op=OP.mult)
            p.linc(eng, "ve")
        p.on("v", v)
        ve_w = p.cnt["ve"]

        def t2f(eng, tt=tt):
            p.w(eng, "ve", ve_w)
            eng.transpose(psWT[:], wall[:, tt * 8:(tt + 1) * 8], b["identf"][:])
            p.linc(eng, "pe")
        p.on("t", t2f)
        pe_tr = p.cnt["pe"]

        def v2(eng, tt=tt):
            p.w(eng, "pe", pe_tr)
            eng.tensor_copy(s_rows[:, tt * 128:(tt + 1) * 128], psWT[:])
            p.linc(eng, "ve")
        p.on("v", v2)
    p.barrier()
    c.close()


def emit_moe_h(p, b, xh_dram, w1_dram, w3_dram, h_dram, s_rows, esel_s,
               blk_row0, e=None, hid=1024, w_row0=0):
    nc = p.nc
    c = ExitStack()
    U = _uniq(p)
    nhc = hid // 128
    w1s = c.enter_context(nc.sbuf_tensor("m_w1" + U, [128, NCH * hid], BF16))
    w3s = c.enter_context(nc.sbuf_tensor("m_w3" + U, [128, NCH * hid], BF16))
    xht = [c.enter_context(nc.sbuf_tensor(f"m_xh{i}" + U, [128, NCH * TB], BF16)) for i in range(2)]
    seb = c.enter_context(nc.sbuf_tensor("m_se" + U, [128, TB], F32)) if e is not None else None
    gs = [c.enter_context(nc.sbuf_tensor(f"m_gs{i}" + U, [128, TB], F32)) for i in range(2)]
    gsb = [c.enter_context(nc.sbuf_tensor(f"m_gsb{i}" + U, [128, TB], BF16)) for i in range(2)]
    us = [c.enter_context(nc.sbuf_tensor(f"m_us{i}" + U, [128, TB], F32)) for i in range(2)]
    hst = c.enter_context(nc.sbuf_tensor("m_h" + U, [128, nhc * TB], BF16))
    psG = [c.enter_context(nc.psum_tensor(f"m_pG{i}" + U, [128, TB], F32)) for i in range(2)]
    psU = [c.enter_context(nc.psum_tensor(f"m_pU{i}" + U, [128, TB], F32)) for i in range(2)]
    psS = c.enter_context(nc.psum_tensor("m_pS" + U, [128, TB], F32)) if e is not None else None

    def g(eng):
        p._w1 = load_fm(p, eng, w1s, w1_dram, hid, w_row0, 0, NCH, hid, sem="dG")
        p._w3 = load_fm(p, eng, w3s, w3_dram, hid, w_row0, 0, NCH, hid, sem="dG")
    p.on("g", g)
    ld_w1, ld_w3 = p._w1, p._w3

    war_x = [0, 0]
    for tb in range(NTB):
        bi = tb % 2
        xt = xht[bi]
        res = {}

        def s(eng, tb=tb, bi=bi, xt=xt):
            p.w(eng, "pe", war_x[bi])
            res["x"] = load_fm(p, eng, xt, xh_dram, TOK, 0, tb * TB, NCH, TB,
                               sem=f"dx{bi}")
        p.on("s", s)
        ld_x = res["x"]

        if e is not None:
            def t0(eng, tb=tb):
                p.w(eng, "ve", p.cnt["ve"])
                eng.matmul(psS[:], esel_s[:, e * 128:(e + 1) * 128],
                           s_rows[:, tb * TB:(tb + 1) * TB], start=True, stop=True)
                p.linc(eng, "pe")
            p.on("t", t0)
            pe_se = p.cnt["pe"]

            def v0(eng, tb=tb):
                p.w(eng, "pe", pe_se)
                eng.tensor_copy(seb[:], psS[:])
                p.fence(eng, "ve")
                p.linc(eng, "ve")
            p.on("v", v0)

        se0 = p.cnt["se"]
        ve0 = p.cnt["ve"]
        ge0 = p.cnt["ge"]
        pe_marks = []

        def t(eng, tb=tb, xt=xt, bi=bi):
            p.w(eng, f"dx{bi}", ld_x)
            p.w(eng, "dG", ld_w3)
            for hc in range(nhc):
                pi = hc % 2
                if hc >= 2:
                    if e is not None:
                        p.w(eng, "ve", ve0 + hc - 1)
                    else:
                        p.w(eng, "se", se0 + hc - 1)
                        p.w(eng, "ve", ve0 + hc - 1)
                for dc in range(NCH):
                    eng.matmul(psG[pi][:], w1s[:, dc * hid + hc * 128: dc * hid + (hc + 1) * 128],
                               xt[:, dc * TB:(dc + 1) * TB],
                               start=(dc == 0), stop=(dc == NCH - 1))
                p.linc(eng, "pe")
                for dc in range(NCH):
                    eng.matmul(psU[pi][:], w3s[:, dc * hid + hc * 128: dc * hid + (hc + 1) * 128],
                               xt[:, dc * TB:(dc + 1) * TB],
                               start=(dc == 0), stop=(dc == NCH - 1))
                pe_marks.append(p.linc(eng, "pe"))
        p.on("t", t)
        war_x[bi] = p.cnt["pe"]

        if e is not None:
            def v(eng, tb=tb):
                for hc in range(nhc):
                    pi = hc % 2
                    p.w(eng, "pe", pe_marks[hc])
                    if hc >= 2:
                        p.w(eng, "se", se0 + hc - 1)
                        p.w(eng, "ge", ge0 + hc - 1)
                    eng.tensor_tensor(out=gs[pi][:], in0=psG[pi][:], in1=seb[:], op=OP.mult)
                    eng.tensor_tensor(out=us[pi][:], in0=psU[pi][:], in1=seb[:], op=OP.mult)
                    p.linc(eng, "ve")
            p.on("v", v)
            assert p.cnt["ve"] == ve0 + nhc
        else:
            def v(eng, tb=tb):
                for hc in range(nhc):
                    pi = hc % 2
                    p.w(eng, "pe", pe_marks[hc])
                    if hc >= 2:
                        p.w(eng, "se", se0 + hc - 1)
                        p.w(eng, "ge", ge0 + hc - 1)
                    p.inc(eng.tensor_copy(us[pi][:], psU[pi][:]), "ve")
            p.on("v", v)

        def a(eng, tb=tb):
            for hc in range(nhc):
                pi = hc % 2
                p.w(eng, "ve", ve0 + hc + 1)
                if hc >= 2:
                    p.w(eng, "ge", ge0 + hc - 1)
                if e is not None:
                    p.inc(eng.activation(out=gsb[pi][:], in_=gs[pi][:], func=AF.Silu), "se")
                else:
                    p.inc(eng.activation(out=gsb[pi][:], in_=psG[pi][:], func=AF.Silu), "se")
        p.on("a", a)
        assert p.cnt["se"] == se0 + nhc

        def gph(eng, tb=tb):
            p.w(eng, "dso", p.cnt["dso"])
            for hc in range(nhc):
                pi = hc % 2
                p.w(eng, "se", se0 + hc + 1)
                p.w(eng, "ve", ve0 + hc + 1)
                p.inc(eng.tensor_tensor(out=hst[:, hc * TB:(hc + 1) * TB],
                                        in0=gsb[pi][:], in1=us[pi][:], op=OP.mult), "ge")
        p.on("g", gph)
        assert p.cnt["ge"] == ge0 + nhc
        ge_h = p.cnt["ge"]

        def s2(eng, tb=tb):
            p.w(eng, "ge", ge_h)
            store_fm(p, eng, hst, h_dram, TOK, blk_row0, tb * TB, nhc, TB, sem="dso")
        p.on("s", s2)
    p.barrier()
    c.close()


def emit_moe_down(p, b, h_dram, w2_blocks, x_dram, out_dram):
    nc = p.nc
    c = ExitStack()
    U = _uniq(p)
    acc = c.enter_context(nc.sbuf_tensor("md_acc" + U, [128, NCH * TOK], F32))
    w2s = c.enter_context(nc.sbuf_tensor("md_w2" + U, [128, 8 * D], BF16))
    hbuf = [c.enter_context(nc.sbuf_tensor(f"md_h{i}" + U, [128, 8 * TB], BF16)) for i in range(2)]
    psD = [c.enter_context(nc.psum_tensor(f"md_p{i}" + U, [128, TB], F32)) for i in range(2)]

    def s(eng):
        p._a = load_fm(p, eng, acc, x_dram, TOK, 0, 0, NCH, TOK, per=2, sem="dac")
    p.on("s", s)
    ld_acc = p._a

    for blk, (w2_dram, w2r0) in enumerate(w2_blocks):
        def g(eng, blk=blk, w2_dram=w2_dram, w2r0=w2r0):
            p.w(eng, "pe", p.cnt["pe"])
            p._w2 = load_fm(p, eng, w2s, w2_dram, D, w2r0, 0, 8, D, sem="dG")
        p.on("g", g)
        ld_w2 = p._w2

        for tb in range(NTB):
            bi = tb % 2

            def s2(eng, blk=blk, tb=tb, bi=bi):
                p.w(eng, "pe", p.cnt["pe"])
                p._h = load_fm(p, eng, hbuf[bi], h_dram, TOK, blk * 1024, tb * TB, 8,
                               TB, per=8, sem=f"dh{bi}")
            p.on("s", s2)
            ld_h = p._h

            ve0 = p.cnt["ve"]
            pe_marks = []

            def t(eng, blk=blk, tb=tb, bi=bi):
                p.w(eng, "dG", ld_w2)
                p.w(eng, f"dh{bi}", ld_h)
                p.w(eng, "dac", ld_acc)
                p.w(eng, "ve", ve0)
                for oc in range(NCH):
                    pi = oc % 2
                    if oc >= 2:
                        p.w(eng, "ve", ve0 + oc - 1)
                    for hk in range(8):
                        eng.matmul(psD[pi][:], w2s[:, hk * D + oc * 128: hk * D + (oc + 1) * 128],
                                   hbuf[bi][:, hk * TB:(hk + 1) * TB],
                                   start=(hk == 0), stop=(hk == 7))
                    pe_marks.append(p.linc(eng, "pe"))
            p.on("t", t)

            def v(eng, blk=blk, tb=tb):
                for oc in range(NCH):
                    pi = oc % 2
                    p.w(eng, "pe", pe_marks[oc])
                    sl = slice(oc * TOK + tb * TB, oc * TOK + (tb + 1) * TB)
                    p.inc(eng.tensor_tensor(out=acc[:, sl], in0=acc[:, sl],
                                            in1=psD[pi][:], op=OP.add), "ve")
                p.fence(eng, "ve")
            p.on("v", v)
            assert p.cnt["ve"] in (ve0 + NCH, ve0 + NCH + 1)

    def s3(eng):
        p.w(eng, "ve", p.cnt["ve"])
        store_fm(p, eng, acc, out_dram, TOK, 0, 0, NCH, TOK, per=2, sem="dso")
    p.on("s", s3)
    p.barrier()
    c.close()



def emit_qkv(p, b, cfg, consts):
    nc = p.nc
    c = ExitStack()
    U = _uniq(p)
    xh = c.enter_context(nc.sbuf_tensor("q_xh" + U, [128, NCH * TOK], BF16))
    cosb = c.enter_context(nc.sbuf_tensor("q_cos" + U, [128, TOK], F32))
    sinb = c.enter_context(nc.sbuf_tensor("q_sin" + U, [128, TOK], F32))
    wc = [c.enter_context(nc.sbuf_tensor(f"q_wc{i}" + U, [128, NCH * 128], BF16)) for i in range(2)]
    qb16 = [c.enter_context(nc.sbuf_tensor(f"q_qb{i}" + U, [128, TB], BF16)) for i in range(2)]
    f1 = [c.enter_context(nc.sbuf_tensor(f"q_f1{i}" + U, [128, TB], F32)) for i in range(2)]
    f2 = [c.enter_context(nc.sbuf_tensor(f"q_f2{i}" + U, [128, TB], F32)) for i in range(2)]
    stag = [c.enter_context(nc.sbuf_tensor(f"q_st{i}" + U, [128, TB], BF16)) for i in range(2)]
    vcfg = cfg.get("v")
    vws = (c.enter_context(nc.sbuf_tensor("q_vw" + U, [128, NCH * vcfg["dv"]], BF16))
           if vcfg else None)
    psP = [c.enter_context(nc.psum_tensor(f"q_pP{i}" + U, [128, TB], F32)) for i in range(2)]
    psR = [c.enter_context(nc.psum_tensor(f"q_pR{i}" + U, [128, TB], F32)) for i in range(2)]

    xh_ncols = cfg["xh_ncols"]
    for r in range(cfg["halves"]):
        def s(eng, r=r):
            p.w(eng, "pe", p.cnt["pe"])
            load_fm(p, eng, xh, cfg["xh"], xh_ncols, 0, r * TOK, NCH, TOK, per=2,
                    sem="dxx")
            src = mkap(cfg["cos"], r * TOK, [[xh_ncols, 128], [1, TOK]])
            p.inc(eng.dma_start(out=cosb[:], in_=src), "dxx", 16)
            src = mkap(cfg["sin"], r * TOK, [[xh_ncols, 128], [1, TOK]])
            p._c = p.inc(eng.dma_start(out=sinb[:], in_=src), "dxx", 16)
        p.on("s", s)
        ld_x = ld_c = p._c

        ve_now = p.cnt["ve"]
        war = {"qb_g": [p.cnt["ge"]] * 2, "qb_pe": [p.cnt["pe"]] * 2,
               "f1": [ve_now] * 2, "f2": [ve_now] * 2,
               "stag": [p.cnt["dsa0"], p.cnt["dsa1"]],
               "psP": [ve_now] * 2, "psR": [ve_now] * 2}
        u = 0
        for job in cfg["rope_jobs"]:
            (wt, wt_ncols, col0, bias_t, bias_c, dst, dst_row0, dst_ncols, per_rank) = job

            def g(eng, u=u, wt=wt, wt_ncols=wt_ncols, col0=col0):
                p.w(eng, "pe", p.cnt["pe"])
                p._w = load_fm(p, eng, wc[u % 2], wt, wt_ncols, 0, col0, NCH, 128,
                               sem=f"dW{u % 2}", per=8)
            p.on("g", g)
            ld_w = p._w
            wci = u % 2

            for tb in range(NTB):
                i = (u * NTB + tb) % 2

                def t(eng, tb=tb, i=i, wci=wci):
                    p.w(eng, "dxx", ld_c)
                    p.w(eng, f"dW{wci}", ld_w)
                    p.w(eng, "ve", war["psP"][i])
                    for dc in range(NCH):
                        eng.matmul(psP[i][:], wc[wci][:, dc * 128:(dc + 1) * 128],
                                   xh[:, dc * TOK + tb * TB: dc * TOK + (tb + 1) * TB],
                                   start=(dc == 0), stop=(dc == NCH - 1))
                    p.linc(eng, "pe")
                p.on("t", t)
                pe_mm = p.cnt["pe"]

                def v(eng, tb=tb, i=i, bias_t=bias_t, bias_c=bias_c):
                    p.w(eng, "pe", max(pe_mm, war["qb_pe"][i]))
                    p.w(eng, "ge", war["qb_g"][i])
                    eng.tensor_scalar(out=qb16[i][:], in0=psP[i][:],
                                      scalar1=bias_t[:, bias_c:bias_c + 1], scalar2=None,
                                      op0=OP.add)
                    p.linc(eng, "ve")
                p.on("v", v)
                ve_qb = p.cnt["ve"]
                war["psP"][i] = ve_qb

                def t2(eng, tb=tb, i=i):
                    p.w(eng, "ve", max(ve_qb, war["psR"][i]))
                    eng.matmul(psR[i][:], consts["Pm"][:], qb16[i][:], start=True, stop=True)
                    p.linc(eng, "pe")
                p.on("t", t2)
                pe_rot = p.cnt["pe"]
                war["qb_pe"][i] = pe_rot

                def v2(eng, tb=tb, i=i):
                    p.w(eng, "pe", pe_rot)
                    p.w(eng, "ve", war["f2"][i])
                    eng.tensor_tensor(out=f2[i][:], in0=psR[i][:],
                                      in1=sinb[:, tb * TB:(tb + 1) * TB], op=OP.mult)
                    p.linc(eng, "ve")
                p.on("v", v2)
                ve_f2 = p.cnt["ve"]
                war["psR"][i] = ve_f2

                def gp(eng, tb=tb, i=i):
                    p.w(eng, "ve", max(ve_qb, war["f1"][i]))
                    p.inc(eng.tensor_tensor(out=f1[i][:], in0=qb16[i][:],
                                             in1=cosb[:, tb * TB:(tb + 1) * TB],
                                             op=OP.mult), "ge")
                p.on("g", gp)
                ge_f1 = p.cnt["ge"]
                war["qb_g"][i] = ge_f1

                def v3(eng, tb=tb, i=i):
                    p.w(eng, "ge", ge_f1)
                    p.w(eng, "ve", ve_f2)
                    p.w(eng, f"dsa{i}", war["stag"][i])
                    eng.tensor_tensor(out=stag[i][:], in0=f1[i][:], in1=f2[i][:], op=OP.add)
                    p.linc(eng, "ve")
                p.on("v", v3)
                ve_st = p.cnt["ve"]
                war["f1"][i] = ve_st
                war["f2"][i] = ve_st

                def s2(eng, tb=tb, i=i, dst=dst, dst_row0=dst_row0,
                       dst_ncols=dst_ncols, per_rank=per_rank, r=r):
                    p.w(eng, "ve", ve_st)
                    row0 = dst_row0 + (per_rank * r)
                    off = (row0 * dst_ncols) + tb * TB
                    dstap = mkap(dst, off, [[dst_ncols, 128], [1, TB]])
                    p._st = p.inc(eng.dma_start(out=dstap, in_=stag[i][:]),
                                  f"dsa{i}", 16)
                p.on("s", s2)
                war["stag"][i] = p._st
            u += 1

        if vcfg:
            def g2(eng, r=r):
                p.w(eng, "pe", p.cnt["pe"])
                p._vw = load_fm(p, eng, vws, vcfg["wv"], vcfg["dv"], 0, 0, NCH,
                                vcfg["dv"], sem="dWv", per=4)
            p.on("g", g2)
            ld_vw = p._vw

            war_pv = [p.cnt["ve"], p.cnt["ve"]]
            war_vst = [p.cnt["dsa0"], p.cnt["dsa1"]]
            for tt in range(NTT):
                for dvb in range(vcfg["dv"] // TB):
                    i = (tt * (vcfg["dv"] // TB) + dvb) % 2

                    def t3(eng, tt=tt, dvb=dvb, i=i):
                        p.w(eng, "dxx", ld_c)
                        p.w(eng, "dWv", ld_vw)
                        p.w(eng, "ve", war_pv[i])
                        for dc in range(NCH):
                            eng.matmul(psP[i][:],
                                       xh[:, dc * TOK + tt * 128: dc * TOK + (tt + 1) * 128],
                                       vws[:, dc * vcfg["dv"] + dvb * TB: dc * vcfg["dv"] + (dvb + 1) * TB],
                                       start=(dc == 0), stop=(dc == NCH - 1))
                        p.linc(eng, "pe")
                    p.on("t", t3)
                    pe_v = p.cnt["pe"]

                    def v4(eng, tt=tt, dvb=dvb, i=i):
                        p.w(eng, "pe", pe_v)
                        p.w(eng, f"dsa{i}", war_vst[i])
                        eng.tensor_tensor(out=stag[i][:], in0=psP[i][:],
                                          in1=vcfg["bv_tile"][:, dvb * TB:(dvb + 1) * TB],
                                          op=OP.add)
                        p.linc(eng, "ve")
                    p.on("v", v4)
                    ve_v = p.cnt["ve"]
                    war_pv[i] = ve_v

                    def s3(eng, tt=tt, dvb=dvb, i=i, r=r):
                        p.w(eng, "ve", ve_v)
                        if vcfg["dv"] == D:
                            off = (r * vcfg["LR"] + D + tt * 128) * D + dvb * TB
                            dstap = mkap(vcfg["kv"], off, [[D, 128], [1, TB]])
                        else:
                            off = r * vcfg["LR"] * D + vcfg["vbase"] + (tt * 128) * vcfg["vstride"]
                            dstap = mkap(vcfg["kv"], off, [[vcfg["vstride"], 128], [1, TB]])
                        war_vst[i] = p.inc(eng.dma_start(out=dstap, in_=stag[i][:]),
                                           f"dsa{i}", 16)
                    p.on("s", s3)
    p.barrier()
    c.close()



def emit_attn(p, b, L, consts):
    nc = p.nc
    c = ExitStack()
    U = _uniq(p)
    kb2 = [c.enter_context(nc.sbuf_tensor(f"at_k{i}" + U, [128, 4096], BF16)) for i in range(2)]
    vb2 = [c.enter_context(nc.sbuf_tensor(f"at_v{i}" + U, [128, 4096], BF16)) for i in range(2)]
    qb2 = [c.enter_context(nc.sbuf_tensor(f"at_q{i}" + U, [128, TOK], BF16)) for i in range(2)]
    rowb = c.enter_context(nc.sbuf_tensor("at_row" + U, [128, 4096], F32))
    exprow = c.enter_context(nc.sbuf_tensor("at_exp" + U, [128, 4096], BF16))
    maxb = c.enter_context(nc.sbuf_tensor("at_max" + U, [128, 16], F32))
    nmax = c.enter_context(nc.sbuf_tensor("at_nm" + U, [128, 1], F32))
    sumc = c.enter_context(nc.sbuf_tensor("at_sum" + U, [128, 1], F32))
    linv = c.enter_context(nc.sbuf_tensor("at_li" + U, [128, 1], F32))
    lrow = c.enter_context(nc.sbuf_tensor("at_lr" + U, [1, 128], F32))
    lbc = c.enter_context(nc.sbuf_tensor("at_lb" + U, [128, 128], F32))
    attnT = [c.enter_context(nc.sbuf_tensor(f"at_aT{i}" + U, [128, 256], BF16)) for i in range(2)]
    obuf = c.enter_context(nc.sbuf_tensor("at_o" + U, [128, TOK], BF16))
    psS = [c.enter_context(nc.psum_tensor(f"at_pS{i}" + U, [128, 256], F32)) for i in range(2)]
    psT = [c.enter_context(nc.psum_tensor(f"at_pT{i}" + U, [128, 256], BF16)) for i in range(2)]
    psO = c.enter_context(nc.psum_tensor("at_pO" + U, [128, 128], F32))
    psLT = c.enter_context(nc.psum_tensor("at_pLT" + U, [1, 128], F32))
    psLB = c.enter_context(nc.psum_tensor("at_pLB" + U, [128, 128], F32))

    LR, vstride, vbase = L["LR"], L["vstride"], L["vbase"]
    war_kv = [0, 0]
    war_ps = [0, 0]
    ld_kv = [0, 0]
    ld_q = [0, 0]
    se_exp_prev = 0
    pe_tr_prev = 0
    ve_ob_prev = 0
    war_ob = 0

    for g in range(L["nkv"]):
        gi = g % 2

        def s(eng, g=g, gi=gi):
            p.w(eng, "pe", war_kv[gi])
            p.w(eng, "dcc", L.get("cc", 0))
            for r in range(2):
                src = mkap(L["kv"], r * LR * D + (g * 128) * D,
                           [[D, 128], [128, 16], [1, 128]])
                dst = mkap(kb2[gi][:], r * 128,
                           [list(kb2[gi][:].ap[0]), [256, 16], [1, 128]])
                p.inc(eng.dma_start(out=dst, in_=src), f"dk{gi}", 16)
                src = mkap(L["kv"], r * LR * D + vbase + L["vcol"] * g,
                           [[vstride, 128], [128 * vstride, 16], [1, 128]])
                dst = mkap(vb2[gi][:], r * 128,
                           [list(vb2[gi][:].ap[0]), [256, 16], [1, 128]])
                ld_kv[gi] = p.inc(eng.dma_start(out=dst, in_=src), f"dk{gi}", 16)
        p.on("s", s)

        for qa in range(L["rep"]):
            h = g * L["rep"] + qa
            qi = h % 2

            def sq(eng, h=h, qi=qi):
                p.w(eng, "pe", p.cnt["pe"])
                src = mkap(L["q_sp"], (h * 128) * TOK, [[TOK, 128], [1, TOK]])
                ld_q[qi] = p.inc(eng.dma_start(out=qb2[qi][:], in_=src), f"dq{qi}", 16)
            p.on("s", sq)

            for j in range(NTT):
                pe_s = []
                ve_e0 = p.cnt["ve"]

                def t(eng, j=j, gi=gi, qi=qi, ve_e0=ve_e0):
                    p.w(eng, f"dk{gi}", ld_kv[gi])
                    p.w(eng, f"dq{qi}", ld_q[qi])
                    for kb in range(j + 1):
                        si = kb % 2
                        if kb < 2:
                            p.w(eng, "ve", war_ps[si])
                        else:
                            p.w(eng, "ve", ve_e0 + kb - 1)
                        pe_s.append(p.inc(
                            eng.matmul(psS[si][:], qb2[qi][:, j * 128:(j + 1) * 128],
                                       kb2[gi][:, kb * 256:(kb + 1) * 256],
                                       start=True, stop=True), "pe"))
                p.on("t", t)

                def v(eng, j=j, ve_e0=ve_e0):
                    p.w(eng, "se", se_exp_prev)
                    assert p.cnt["ve"] == ve_e0
                    for kb in range(j + 1):
                        si = kb % 2
                        p.w(eng, "pe", pe_s[kb])
                        sl = rowb[:, kb * 256:(kb + 1) * 256]
                        if kb == j:
                            eng.tensor_tensor(out=sl, in0=psS[si][:],
                                              in1=consts["mask"][:], op=OP.add)
                            p.fence(eng, "ve")
                            war_ps[si] = p.inc(eng.tensor_scalar(
                                out=sl, in0=sl, scalar1=SCALE, scalar2=-3.0e38,
                                op0=OP.mult, op1=OP.max,
                                accum_out=maxb[:, kb:kb + 1]), "ve")
                        else:
                            war_ps[si] = p.inc(eng.tensor_scalar(
                                out=sl, in0=psS[si][:], scalar1=SCALE, scalar2=-3.0e38,
                                op0=OP.mult, op1=OP.max,
                                accum_out=maxb[:, kb:kb + 1]), "ve")
                    p.fence(eng, "ve")
                    eng.reduce_max(nmax[:, 0:1], maxb[:, 0:j + 1], axis=AX.X, negate=True)
                    p.fence(eng, "ve")
                    p.linc(eng, "ve")
                p.on("v", v)
                ve_nmax = p.cnt["ve"]

                def a(eng, j=j):
                    p.w(eng, "ve", ve_nmax)
                    p.w(eng, "pe", pe_tr_prev)
                    eng.activation(out=exprow[:, 0:256 * (j + 1)],
                                   in_=rowb[:, 0:256 * (j + 1)], func=AF.Exp,
                                   bias=nmax[:, 0:1], accum_out=sumc[:, 0:1])
                    p.linc(eng, "se")
                p.on("a", a)
                se_exp = p.cnt["se"]
                se_exp_prev = se_exp

                def v2(eng, j=j):
                    p.w(eng, "se", se_exp)
                    eng.reciprocal(linv[:, 0:1], sumc[:, 0:1])
                    p.fence(eng, "ve")
                    p.linc(eng, "ve")
                p.on("v", v2)
                ve_li = p.cnt["ve"]

                def t2(eng, j=j):
                    p.w(eng, "ve", ve_li)
                    eng.matmul(psLT[:], linv[:, 0:1], b["identf"][:], start=True, stop=True)
                    p.linc(eng, "pe")
                p.on("t", t2)
                pe_lt = p.cnt["pe"]

                def v3(eng, j=j):
                    p.w(eng, "pe", pe_lt)
                    eng.tensor_copy(lrow[:], psLT[:])
                    p.linc(eng, "ve")
                p.on("v", v3)
                ve_lr = p.cnt["ve"]

                def t3(eng, j=j):
                    p.w(eng, "ve", ve_lr)
                    eng.matmul(psLB[:], b["ones1"][:], lrow[:], start=True, stop=True)
                    p.linc(eng, "pe")
                p.on("t", t3)
                pe_lb = p.cnt["pe"]

                def v4(eng, j=j):
                    p.w(eng, "pe", pe_lb)
                    eng.tensor_copy(lbc[:], psLB[:])
                    p.linc(eng, "ve")
                p.on("v", v4)
                ve_base = p.cnt["ve"]

                tr_marks = []

                def t4(eng, j=j, gi=gi):
                    p.w(eng, "se", se_exp)
                    p.w(eng, "ve", ve_ob_prev)
                    for kb in range(j + 1):
                        ti = kb % 2
                        if kb >= 2:
                            p.w(eng, "ve", ve_base + kb - 1)
                        eng.transpose(psT[ti][:, 0:128],
                                      exprow[:, kb * 256:kb * 256 + 128], b["identb"][:])
                        tr_marks.append(p.inc(
                            eng.transpose(psT[ti][:, 128:256],
                                          exprow[:, kb * 256 + 128:kb * 256 + 256],
                                          b["identb"][:]), "pe"))
                        p.w(eng, "ve", ve_base + kb + 1)
                        eng.matmul(psO[:], vb2[gi][:, (2 * kb) * 128:(2 * kb + 1) * 128],
                                   attnT[ti][:, 0:128],
                                   start=(kb == 0), stop=False)
                        eng.matmul(psO[:], vb2[gi][:, (2 * kb + 1) * 128:(2 * kb + 2) * 128],
                                   attnT[ti][:, 128:256],
                                   start=False, stop=(kb == j))
                    p.linc(eng, "pe")
                p.on("t", t4)
                pe_av = p.cnt["pe"]
                pe_tr_prev = pe_av

                def v5(eng, j=j):
                    for kb in range(j + 1):
                        ti = kb % 2
                        p.w(eng, "pe", tr_marks[kb])
                        p.inc(eng.tensor_copy(attnT[ti][:], psT[ti][:]), "ve")
                p.on("v", v5)
                assert p.cnt["ve"] == ve_base + j + 1

                def v6(eng, j=j):
                    p.w(eng, "pe", pe_av)
                    p.w(eng, "ve", ve_base)
                    p.w(eng, "dos", war_ob)
                    eng.tensor_tensor(out=obuf[:, j * 128:(j + 1) * 128],
                                      in0=lbc[:], in1=psO[:], op=OP.mult)
                    p.linc(eng, "ve")
                p.on("v", v6)
                ve_ob_prev = p.cnt["ve"]

            def so(eng, h=h):
                p.w(eng, "ve", ve_ob_prev)
                dst = mkap(L["o_sp"], (h * 128) * TOK, [[TOK, 128], [1, TOK]])
                p._st = p.inc(eng.dma_start(out=dst, in_=obuf[:]), "dos", 16)
            p.on("s", so)
            war_ob = p._st

        war_kv[gi] = p.cnt["pe"]
    p.barrier()
    c.close()



def emit_oproj(p, b, L, consts):
    nc = p.nc
    c = ExitStack()
    U = _uniq(p)
    osp = c.enter_context(nc.sbuf_tensor("op_o" + U, [128, NCH * TOK], BF16))
    woc = [c.enter_context(nc.sbuf_tensor(f"op_w{i}" + U, [128, NCH * 128], BF16)) for i in range(2)]
    xc = [c.enter_context(nc.sbuf_tensor(f"op_x{i}" + U, [128, TB], F32)) for i in range(2)]
    ob = [c.enter_context(nc.sbuf_tensor(f"op_b{i}" + U, [128, TB], F32)) for i in range(2)]
    psX = [c.enter_context(nc.psum_tensor(f"op_p{i}" + U, [128, TB], F32)) for i in range(2)]

    def s(eng):
        p._o = load_fm(p, eng, osp, L["o_sp"], TOK, 0, 0, NCH, TOK, per=2, sem="dos")
    p.on("s", s)
    ld_o = p._o

    war_w = [0, 0]
    war_ps = [0, 0]
    war_x = [0, 0]
    war_ob = [0, 0]
    for oc in range(NCH):
        wi = oc % 2

        def g(eng, oc=oc, wi=wi):
            p.w(eng, "pe", war_w[wi])
            p._w = load_fm(p, eng, woc[wi], L["wo"], D, 0, oc * 128, NCH, 128,
                           sem=f"dW{wi}", per=8)
        p.on("g", g)
        ld_w = p._w

        for tb in range(NTB):
            i = (oc * NTB + tb) % 2

            def s2(eng, oc=oc, tb=tb, i=i):
                p.w(eng, "ve", war_x[i])
                src = mkap(L["x_res"], (oc * 128) * TOK + tb * TB, [[TOK, 128], [1, TB]])
                p._x = p.inc(eng.dma_start(out=xc[i][:], in_=src), f"dxa{i}", 16)
            p.on("s", s2)
            ld_x = p._x

            def t(eng, oc=oc, tb=tb, i=i, wi=wi):
                p.w(eng, "dos", ld_o)
                p.w(eng, f"dW{wi}", ld_w)
                p.w(eng, "ve", war_ps[i])
                for hc in range(NCH):
                    eng.matmul(psX[i][:], woc[wi][:, hc * 128:(hc + 1) * 128],
                               osp[:, hc * TOK + tb * TB: hc * TOK + (tb + 1) * TB],
                               start=(hc == 0), stop=(hc == NCH - 1))
                p.linc(eng, "pe")
            p.on("t", t)
            pe_mm = p.cnt["pe"]

            def v(eng, oc=oc, tb=tb, i=i):
                p.w(eng, "pe", pe_mm)
                p.w(eng, f"dxa{i}", ld_x)
                p.w(eng, f"dsa{i}", war_ob[i])
                eng.scalar_tensor_tensor(out=ob[i][:], in0=psX[i][:],
                                         scalar=consts["bo"][:, oc:oc + 1],
                                         in1=xc[i][:], op0=OP.add, op1=OP.add)
                p.linc(eng, "ve")
            p.on("v", v)
            ve_o = p.cnt["ve"]
            war_ps[i] = ve_o
            war_x[i] = ve_o

            def s3(eng, oc=oc, tb=tb, i=i):
                p.w(eng, "ve", ve_o)
                dst = mkap(L["x1pre"], (oc * 128) * TOK + tb * TB, [[TOK, 128], [1, TB]])
                p._s = p.inc(eng.dma_start(out=dst, in_=ob[i][:]), f"dsa{i}", 16)
            p.on("s", s3)
            war_ob[i] = p._s
        war_w[wi] = p.cnt["pe"]
    p.barrier()
    c.close()


def emit_epilogue(p):
    def s(eng):
        for n in p.DMA_SEMS + ("pe", "ve", "se", "ge"):
            p.w(eng, n, p.cnt[n])
    p.on("s", s)



def _inp(nc, name, shape, dt=F32):
    return nc.dram_tensor(name, shape, dt, kind="ExternalInput").ap()


def _out(nc, name, shape, dt=F32):
    return nc.dram_tensor(name, shape, dt, kind="ExternalOutput").ap()


def _int(nc, name, shape, dt=BF16):
    return nc.dram_tensor(name, shape, dt, kind="Internal").ap()


def _load_consts(p, pairs):
    def s(eng):
        for tile, ap in pairs:
            p.inc(eng.dma_start(out=tile[:], in_=ap), "dcn", 16)
    p.on("s", s)
    p.barrier()


def build_launchA():
    p = Prog("A")
    nc = p.nc
    x_own = _inp(nc, "x_own", [D, TOK])
    x_pair = _inp(nc, "x_pair", [D, 2 * TOK])
    wq = _inp(nc, "wq", [D, D])
    wk = _inp(nc, "wk", [D, 512])
    wv = _inp(nc, "wv", [D, 512])
    wo = _inp(nc, "wo", [D, D])
    bqc = _inp(nc, "bqc", [128, 16])
    bkc = _inp(nc, "bkc", [128, 4])
    bvb = _inp(nc, "bvb", [128, 512])
    boc = _inp(nc, "boc", [128, 16])
    lnwc = _inp(nc, "lnwc", [128, 16])
    cosq = _inp(nc, "cosq", [128, TOK])
    sinq = _inp(nc, "sinq", [128, TOK])
    cosp = _inp(nc, "cosp", [128, 2 * TOK])
    sinp = _inp(nc, "sinp", [128, 2 * TOK])
    maskc = _inp(nc, "maskc", [128, 256])
    pmb = _inp(nc, "pmb", [128, 128], BF16)

    x1 = _out(nc, "x1", [D, TOK])
    xh1 = _out(nc, "xh1", [D, TOK], BF16)

    h_own = _int(nc, "h_own", [D, TOK])
    h_pair = _int(nc, "h_pair", [D, 2 * TOK])
    kv = _int(nc, "kv", [2 * 1024, D])
    q_sp = _int(nc, "q_sp", [D, TOK])
    o_sp = _int(nc, "o_sp", [D, TOK])
    x1pre = _int(nc, "x1pre", [D, TOK], F32)

    b = alloc_common(p)
    cst = {}
    for nm, sh, dt in [("bq", [128, 16], F32), ("bk", [128, 4], F32),
                       ("bo", [128, 16], F32), ("lnw", [128, 16], F32),
                       ("bv", [128, 512], F32), ("Pm", [128, 128], BF16),
                       ("mask", [128, 256], F32)]:
        cst[nm] = p.ctx.enter_context(nc.sbuf_tensor("c_" + nm, sh, dt))

    p.start()
    emit_prologue(p, b)
    _load_consts(p, [(cst["bq"], bqc), (cst["bk"], bkc), (cst["bo"], boc),
                     (cst["lnw"], lnwc), (cst["bv"], bvb), (cst["Pm"], pmb),
                     (cst["mask"], maskc)])
    if upto >= 1:
        emit_rms_phase(p, b, x_own, h_own, cst["lnw"], ncols=TOK)
    emit_rms_phase(p, b, x_pair, h_pair, cst["lnw"], ncols=2 * TOK)
    if upto >= 2:
        emit_qkv(p, b, dict(
            xh=h_own, xh_ncols=TOK, halves=1,
        rope_jobs=[(wq, D, h * 128, cst["bq"], h, q_sp, h * 128, TOK, 0)
                   for h in range(16)],
        v=None, cos=cosq, sin=sinq), dict(Pm=cst["Pm"]))
    emit_qkv(p, b, dict(
        xh=h_pair, xh_ncols=2 * TOK, halves=2,
        rope_jobs=[(wk, 512, kc * 128, cst["bk"], kc, kv, kc * 128, D, 1024)
                   for kc in range(4)],
        v=dict(wv=wv, dv=512, bv_tile=cst["bv"], kv=kv, LR=1024,
               vbase=512 * D, vstride=512),
        cos=cosp, sin=sinp), dict(Pm=cst["Pm"]))
    emit_attn(p, b, dict(kv=kv, q_sp=q_sp, o_sp=o_sp, LR=1024, nkv=4, rep=4,
                         vstride=512, vbase=512 * D, vcol=128), dict(mask=cst["mask"]))
    emit_oproj(p, b, dict(o_sp=o_sp, wo=wo, x_res=x_own, x1pre=x1pre),
               dict(bo=cst["bo"]))
    if upto >= 6:
        emit_ln_phase(p, b, x1pre, dst_dram=x1, dsth_dram=xh1)
    emit_epilogue(p)
    p.close()
    return p


def build_launchB():
    p = Prog("B")
    nc = p.nc
    x1 = _inp(nc, "x1", [D, TOK])
    xh1 = _inp(nc, "xh1", [D, TOK], BF16)
    gate_t = _inp(nc, "gate_t", [D, 8])
    ebias = _inp(nc, "ebias", [128, 8])
    w1t = _inp(nc, "w1t", [8 * D, 1024])
    w2t = _inp(nc, "w2t", [8 * 1024, D])
    w3t = _inp(nc, "w3t", [8 * D, 1024])
    sw1t = _inp(nc, "sw1t", [D, 1024])
    sw2t = _inp(nc, "sw2t", [1024, D])
    sw3t = _inp(nc, "sw3t", [D, 1024])
    esel = _inp(nc, "esel", [8, 1024])

    x2 = _out(nc, "x2", [D, TOK])
    xh2 = _out(nc, "xh2", [D, TOK], BF16)
    h_moe = _int(nc, "h_moe", [9 * 1024, TOK])
    x2pre = _int(nc, "x2pre", [D, TOK], F32)

    b = alloc_common(p)
    s_rows = p.ctx.enter_context(nc.sbuf_tensor("srows", [8, TOK], F32))
    wall = p.ctx.enter_context(nc.sbuf_tensor("wall", [128, NTT * 8], F32))
    ebias_s = p.ctx.enter_context(nc.sbuf_tensor("ebias_s", [128, 8], F32))
    esel_s = p.ctx.enter_context(nc.sbuf_tensor("esel_s", [8, 1024], F32))

    p.start()
    emit_prologue(p, b)
    _load_consts(p, [(ebias_s, ebias), (esel_s, esel)])
    emit_moe_routing(p, b, x1, gate_t, ebias_s, s_rows, wall)
    emit_moe_h(p, b, xh1, sw1t, sw3t, h_moe, s_rows, esel_s, 8 * 1024, e=None)
    for e in range(8):
        emit_moe_h(p, b, xh1, w1t, w3t, h_moe, s_rows, esel_s, e * 1024,
                   e=e, w_row0=e * D)
    if upto >= 10:
        emit_moe_down(p, b, h_moe,
                      [(w2t, e * 1024) for e in range(8)] + [(sw2t, 0)],
                      x1, x2pre)
    if upto >= 11:
        emit_ln_phase(p, b, x2pre, dst_dram=x2, dsth_dram=xh2)
    emit_epilogue(p)
    p.close()
    return p


def build_launchC():
    p = Prog("C")
    nc = p.nc
    x2 = _inp(nc, "x2", [D, TOK])
    xh2 = _inp(nc, "xh2", [D, TOK], BF16)
    xh2p = _inp(nc, "xh2p", [D, 2 * TOK], BF16)
    wq = _inp(nc, "wq", [D, D])
    wk = _inp(nc, "wk", [D, D])
    wv = _inp(nc, "wv", [D, D])
    wo = _inp(nc, "wo", [D, D])
    bqc = _inp(nc, "bqc", [128, 16])
    bkc = _inp(nc, "bkc", [128, 16])
    bvb = _inp(nc, "bvb", [128, D])
    boc = _inp(nc, "boc", [128, 16])
    cosq = _inp(nc, "cosq", [128, TOK])
    sinq = _inp(nc, "sinq", [128, TOK])
    cosp = _inp(nc, "cosp", [128, 2 * TOK])
    sinp = _inp(nc, "sinp", [128, 2 * TOK])
    maskc = _inp(nc, "maskc", [128, 256])
    pmb = _inp(nc, "pmb", [128, 128], BF16)

    x3 = _out(nc, "x3", [D, TOK])
    xh3 = _out(nc, "xh3", [D, TOK], BF16)
    kv = _int(nc, "kv", [2 * 4096, D])
    q_sp = _int(nc, "q_sp", [D, TOK])
    o_sp = _int(nc, "o_sp", [D, TOK])
    x3pre = _int(nc, "x3pre", [D, TOK], F32)

    b = alloc_common(p)
    cst = {}
    for nm, sh, dt in [("bq", [128, 16], F32), ("bk", [128, 16], F32),
                       ("bo", [128, 16], F32), ("bv", [128, D], F32),
                       ("Pm", [128, 128], BF16), ("mask", [128, 256], F32)]:
        cst[nm] = p.ctx.enter_context(nc.sbuf_tensor("c_" + nm, sh, dt))

    p.start()
    emit_prologue(p, b)
    _load_consts(p, [(cst["bq"], bqc), (cst["bk"], bkc), (cst["bo"], boc),
                     (cst["bv"], bvb), (cst["Pm"], pmb), (cst["mask"], maskc)])
    if upto >= 12:
        emit_qkv(p, b, dict(
            xh=xh2, xh_ncols=TOK, halves=1,
        rope_jobs=[(wq, D, h * 128, cst["bq"], h, q_sp, h * 128, TOK, 0)
                   for h in range(16)],
        v=None, cos=cosq, sin=sinq), dict(Pm=cst["Pm"]))
    emit_qkv(p, b, dict(
        xh=xh2p, xh_ncols=2 * TOK, halves=2,
        rope_jobs=[(wk, D, kc * 128, cst["bk"], kc, kv, kc * 128, D, 4096)
                   for kc in range(16)],
        v=dict(wv=wv, dv=D, bv_tile=cst["bv"], kv=kv, LR=4096,
               vbase=D * D, vstride=D),
        cos=cosp, sin=sinp), dict(Pm=cst["Pm"]))
    emit_attn(p, b, dict(kv=kv, q_sp=q_sp, o_sp=o_sp, LR=4096, nkv=16, rep=1,
                         vstride=D, vbase=D * D, vcol=128), dict(mask=cst["mask"]))
    emit_oproj(p, b, dict(o_sp=o_sp, wo=wo, x_res=x2, x1pre=x3pre),
               dict(bo=cst["bo"]))
    if upto >= 16:
        emit_ln_phase(p, b, x3pre, dst_dram=x3, dsth_dram=xh3)
    emit_epilogue(p)
    p.close()
    return p


def build_launchD():
    p = Prog("D")
    nc = p.nc
    x3 = _inp(nc, "x3", [D, TOK])
    xh3 = _inp(nc, "xh3", [D, TOK], BF16)
    wgt = _inp(nc, "wgt", [D, 8192])
    wut = _inp(nc, "wut", [D, 8192])
    wdt = _inp(nc, "wdt", [8192, D])

    y = _out(nc, "y", [TOK, D])
    h_ffn = _int(nc, "h_ffn", [8192, TOK])
    x4pre = _int(nc, "x4pre", [D, TOK], F32)

    b = alloc_common(p)
    p.start()
    emit_prologue(p, b)
    if upto >= 17:
        emit_ffn_h(p, b, xh3, wgt, wut, h_ffn)
    if upto >= 18:
        emit_ffn_down(p, b, h_ffn, wdt, x3, x4pre)
    if upto >= 19:
        emit_ln_phase(p, b, x4pre, y_dram=y)
    emit_epilogue(p)
    p.close()
    return p



def _fm(a):
    return np.ascontiguousarray(a.T)


def stage_A(inputs):
    maps = []
    wq = np.ascontiguousarray(np.asarray(inputs["L0_attn_q_project"], np.float32).T)
    wk = np.ascontiguousarray(np.asarray(inputs["L0_attn_k_project"], np.float32).T)
    wv = np.ascontiguousarray(np.asarray(inputs["L0_attn_v_project"], np.float32).T)
    wo = np.ascontiguousarray(np.asarray(inputs["L0_attn_o_project"], np.float32).T)
    bq = col_layout(np.asarray(inputs["L0_attn_q_bias"], np.float32))
    bk = col_layout(np.asarray(inputs["L0_attn_k_bias"], np.float32))
    bv = bcast_rows(np.asarray(inputs["L0_attn_v_bias"], np.float32))
    bo = np.zeros((128, 16), np.float32)
    lnw = col_layout(np.asarray(inputs["L0_attn_ln_weight"], np.float32))
    pm0, pm1 = pos_map(0), pos_map(1)
    cosp, sinp = rope_tables(np.concatenate([pm0, pm1]))
    x0 = np.asarray(inputs["L0_attn_x"], np.float32)
    for c in range(8):
        par, bat = c // 4, c % 4
        pm = pos_map(par)
        cq, sq = rope_tables(pm)
        maps.append(dict(
            x_own=_fm(x0[bat][pm]),
            x_pair=_fm(np.concatenate([x0[bat][pm0], x0[bat][pm1]], 0)),
            wq=wq, wk=wk, wv=wv, wo=wo, bqc=bq, bkc=bk, bvb=bv, boc=bo,
            lnwc=lnw, cosq=cq, sinq=sq, cosp=cosp, sinp=sinp,
            maskc=causal_mask(par), pmb=perm_matrix()))
    return maps


def stage_B(inputs, resA):
    gate_t = np.ascontiguousarray(np.asarray(inputs["L0_ffn_gate"], np.float32).T)
    eb = bcast_rows(np.asarray(inputs["L0_ffn_expert_bias"], np.float32))
    w1 = np.asarray(inputs["L0_ffn_w1"], np.float32)
    w2 = np.asarray(inputs["L0_ffn_w2"], np.float32)
    w3 = np.asarray(inputs["L0_ffn_w3"], np.float32)
    w1t = np.ascontiguousarray(w1.transpose(0, 2, 1)).reshape(8 * D, 1024)
    w2t = np.ascontiguousarray(w2.transpose(0, 2, 1)).reshape(8 * 1024, D)
    w3t = np.ascontiguousarray(w3.transpose(0, 2, 1)).reshape(8 * D, 1024)
    sw1t = np.ascontiguousarray(np.asarray(inputs["L0_ffn_shared_w1"], np.float32).T)
    sw2t = np.ascontiguousarray(np.asarray(inputs["L0_ffn_shared_w2"], np.float32).T)
    sw3t = np.ascontiguousarray(np.asarray(inputs["L0_ffn_shared_w3"], np.float32).T)
    es = esel_matrix()
    maps = []
    for c in range(8):
        maps.append(dict(
            x1=np.asarray(resA[c]["x1"]), xh1=np.asarray(resA[c]["xh1"]),
            gate_t=gate_t, ebias=eb, w1t=w1t, w2t=w2t, w3t=w3t,
            sw1t=sw1t, sw2t=sw2t, sw3t=sw3t, esel=es))
    return maps


def stage_C(inputs, resB):
    wq = np.ascontiguousarray(np.asarray(inputs["L1_attn_Wq"], np.float32))
    wk = np.ascontiguousarray(np.asarray(inputs["L1_attn_Wk"], np.float32))
    wv = np.ascontiguousarray(np.asarray(inputs["L1_attn_Wv"], np.float32))
    wo = np.ascontiguousarray(np.asarray(inputs["L1_attn_Wo"], np.float32))
    bq = col_layout(np.asarray(inputs["L1_attn_bq"], np.float32))
    bk = col_layout(np.asarray(inputs["L1_attn_bk"], np.float32))
    bv = bcast_rows(np.asarray(inputs["L1_attn_bv"], np.float32))
    bo = col_layout(np.asarray(inputs["L1_attn_bo"], np.float32))
    pm0, pm1 = pos_map(0), pos_map(1)
    cosp, sinp = rope_tables(np.concatenate([pm0, pm1]))
    maps = []
    for c in range(8):
        par, bat = c // 4, c % 4
        pm = pos_map(par)
        cq, sq = rope_tables(pm)
        xh2p = np.concatenate([np.asarray(resB[bat]["xh2"]),
                               np.asarray(resB[bat + 4]["xh2"])], 1)
        maps.append(dict(
            x2=np.asarray(resB[c]["x2"]), xh2=np.asarray(resB[c]["xh2"]),
            xh2p=np.ascontiguousarray(xh2p),
            wq=wq, wk=wk, wv=wv, wo=wo, bqc=bq, bkc=bk, bvb=bv, boc=bo,
            cosq=cq, sinq=sq, cosp=cosp, sinp=sinp,
            maskc=causal_mask(par), pmb=perm_matrix()))
    return maps


def stage_D(inputs, resC):
    wgt = np.ascontiguousarray(np.asarray(inputs["L1_ffn_w_gate"], np.float32).T)
    wut = np.ascontiguousarray(np.asarray(inputs["L1_ffn_w_up"], np.float32).T)
    wdt = np.ascontiguousarray(np.asarray(inputs["L1_ffn_w_down"], np.float32).T)
    maps = []
    for c in range(8):
        maps.append(dict(x3=np.asarray(resC[c]["x3"]),
                         xh3=np.asarray(resC[c]["xh3"]),
                         wgt=wgt, wut=wut, wdt=wdt))
    return maps



def emit_collective(p, src_ap, dst_ap):
    def g(eng):
        for s in ("dsa0", "dsa1"):
            p.w(eng, s, p.cnt[s])
        p.inc(eng.collective_compute(
            "AllGather", OP.bypass,
            replica_groups=[[0, 4], [1, 5], [2, 6], [3, 7]],
            ins=[src_ap.opt()], outs=[dst_ap.opt()]), "dcc", 1)
    p.on("g", g)
    return p.cnt["dcc"]


def build_fused(upto=99):
    p = Prog("F")
    nc = p.nc
    x_own = _inp(nc, "x_own", [D, TOK])
    wq0 = _inp(nc, "wq0", [D, D])
    wk0 = _inp(nc, "wk0", [D, 512])
    wv0 = _inp(nc, "wv0", [D, 512])
    wo0 = _inp(nc, "wo0", [D, D])
    bqc0 = _inp(nc, "bqc0", [128, 16])
    bkc0 = _inp(nc, "bkc0", [128, 4])
    bvb0 = _inp(nc, "bvb0", [128, 512])
    boc0 = _inp(nc, "boc0", [128, 16])
    lnwc = _inp(nc, "lnwc", [128, 16])
    gate_t = _inp(nc, "gate_t", [D, 8])
    ebias = _inp(nc, "ebias", [128, 8])
    w1t = _inp(nc, "w1t", [8 * D, 1024])
    w2t = _inp(nc, "w2t", [8 * 1024, D])
    w3t = _inp(nc, "w3t", [8 * D, 1024])
    sw1t = _inp(nc, "sw1t", [D, 1024])
    sw2t = _inp(nc, "sw2t", [1024, D])
    sw3t = _inp(nc, "sw3t", [D, 1024])
    esel = _inp(nc, "esel", [8, 1024])
    wq1 = _inp(nc, "wq1", [D, D])
    wk1 = _inp(nc, "wk1", [D, D])
    wv1 = _inp(nc, "wv1", [D, D])
    wo1 = _inp(nc, "wo1", [D, D])
    bqc1 = _inp(nc, "bqc1", [128, 16])
    bkc1 = _inp(nc, "bkc1", [128, 16])
    bvb1 = _inp(nc, "bvb1", [128, D])
    boc1 = _inp(nc, "boc1", [128, 16])
    wgt = _inp(nc, "wgt", [D, 8192])
    wut = _inp(nc, "wut", [D, 8192])
    wdt = _inp(nc, "wdt", [8192, D])
    cosq = _inp(nc, "cosq", [128, TOK])
    sinq = _inp(nc, "sinq", [128, TOK])
    maskc = _inp(nc, "maskc", [128, 256])
    pmb = _inp(nc, "pmb", [128, 128], BF16)

    y = _out(nc, "y", [TOK, D])

    h_own = _int(nc, "h_own", [D, TOK])
    kv0_loc = _int(nc, "kv0_loc", [1024, D])
    kv0 = _int(nc, "kv0", [2 * 1024, D])
    q0 = _int(nc, "q0", [D, TOK])
    o0 = _int(nc, "o0", [D, TOK])
    x1pre = _int(nc, "x1pre", [D, TOK], F32)
    x1 = _int(nc, "x1", [D, TOK], F32)
    xh1 = _int(nc, "xh1", [D, TOK])
    h_moe = _int(nc, "h_moe", [9 * 1024, TOK])
    x2pre = _int(nc, "x2pre", [D, TOK], F32)
    x2 = _int(nc, "x2", [D, TOK], F32)
    xh2 = _int(nc, "xh2", [D, TOK])
    kv1_loc = _int(nc, "kv1_loc", [4096, D])
    kv1 = _int(nc, "kv1", [2 * 4096, D])
    q1 = _int(nc, "q1", [D, TOK])
    o1 = _int(nc, "o1", [D, TOK])
    x3pre = _int(nc, "x3pre", [D, TOK], F32)
    x3 = _int(nc, "x3", [D, TOK], F32)
    xh3 = _int(nc, "xh3", [D, TOK])
    h_ffn = _int(nc, "h_ffn", [8192, TOK])
    x4pre = _int(nc, "x4pre", [D, TOK], F32)

    b = alloc_common(p)
    cst = {}
    for nm, sh, dt in [("bq0", [128, 16], F32), ("bk0", [128, 4], F32),
                       ("bo0", [128, 16], F32), ("lnw", [128, 16], F32),
                       ("bv0", [128, 512], F32), ("Pm", [128, 128], BF16),
                       ("mask", [128, 256], F32), ("ebias", [128, 8], F32),
                       ("esel", [8, 1024], F32), ("bq1", [128, 16], F32),
                       ("bk1", [128, 16], F32), ("bo1", [128, 16], F32),
                       ("bv1", [128, D], F32)]:
        cst[nm] = p.ctx.enter_context(nc.sbuf_tensor("c_" + nm, sh, dt))
    s_rows = p.ctx.enter_context(nc.sbuf_tensor("srows", [8, TOK], F32))
    wall = p.ctx.enter_context(nc.sbuf_tensor("wall", [128, NTT * 8], F32))

    p.start()
    emit_prologue(p, b)
    _load_consts(p, [(cst["bq0"], bqc0), (cst["bk0"], bkc0), (cst["bo0"], boc0),
                     (cst["lnw"], lnwc), (cst["bv0"], bvb0), (cst["Pm"], pmb),
                     (cst["mask"], maskc), (cst["ebias"], ebias),
                     (cst["esel"], esel), (cst["bq1"], bqc1), (cst["bk1"], bkc1),
                     (cst["bo1"], boc1), (cst["bv1"], bvb1)])

    emit_rms_phase(p, b, x_own, h_own, cst["lnw"], ncols=TOK)
    emit_qkv(p, b, dict(
        xh=h_own, xh_ncols=TOK, halves=1,
        rope_jobs=[(wq0, D, h * 128, cst["bq0"], h, q0, h * 128, TOK, 0)
                   for h in range(16)]
                + [(wk0, 512, kc * 128, cst["bk0"], kc, kv0_loc, kc * 128, D, 0)
                   for kc in range(4)],
        v=dict(wv=wv0, dv=512, bv_tile=cst["bv0"], kv=kv0_loc, LR=1024,
               vbase=512 * D, vstride=512),
        cos=cosq, sin=sinq), dict(Pm=cst["Pm"]))
    if upto >= 3:
        cc0 = emit_collective(p, kv0_loc, kv0)
    if upto >= 4:
        emit_attn(p, b, dict(kv=kv0, q_sp=q0, o_sp=o0, LR=1024, nkv=4, rep=4,
                         vstride=512, vbase=512 * D, vcol=128, cc=cc0),
              dict(mask=cst["mask"]))
    if upto >= 5:
        emit_oproj(p, b, dict(o_sp=o0, wo=wo0, x_res=x_own, x1pre=x1pre),
               dict(bo=cst["bo0"]))
    emit_ln_phase(p, b, x1pre, dst_dram=x1, dsth_dram=xh1)

    if upto >= 7:
        emit_moe_routing(p, b, x1, gate_t, cst["ebias"], s_rows, wall)
    if upto >= 8:
        emit_moe_h(p, b, xh1, sw1t, sw3t, h_moe, s_rows, cst["esel"], 8 * 1024, e=None)
    if upto >= 9:
        for e in range(8):
            emit_moe_h(p, b, xh1, w1t, w3t, h_moe, s_rows, cst["esel"], e * 1024,
                       e=e, w_row0=e * D)
    emit_moe_down(p, b, h_moe,
                  [(w2t, e * 1024) for e in range(8)] + [(sw2t, 0)],
                  x1, x2pre)
    emit_ln_phase(p, b, x2pre, dst_dram=x2, dsth_dram=xh2)

    emit_qkv(p, b, dict(
        xh=xh2, xh_ncols=TOK, halves=1,
        rope_jobs=[(wq1, D, h * 128, cst["bq1"], h, q1, h * 128, TOK, 0)
                   for h in range(16)]
                + [(wk1, D, kc * 128, cst["bk1"], kc, kv1_loc, kc * 128, D, 0)
                   for kc in range(16)],
        v=dict(wv=wv1, dv=D, bv_tile=cst["bv1"], kv=kv1_loc, LR=4096,
               vbase=D * D, vstride=D),
        cos=cosq, sin=sinq), dict(Pm=cst["Pm"]))
    if upto >= 13:
        cc1 = emit_collective(p, kv1_loc, kv1)
    if upto >= 14:
        emit_attn(p, b, dict(kv=kv1, q_sp=q1, o_sp=o1, LR=4096, nkv=16, rep=1,
                         vstride=D, vbase=D * D, vcol=128, cc=cc1),
              dict(mask=cst["mask"]))
    if upto >= 15:
        emit_oproj(p, b, dict(o_sp=o1, wo=wo1, x_res=x2, x1pre=x3pre),
               dict(bo=cst["bo1"]))
    emit_ln_phase(p, b, x3pre, dst_dram=x3, dsth_dram=xh3)

    emit_ffn_h(p, b, xh3, wgt, wut, h_ffn)
    emit_ffn_down(p, b, h_ffn, wdt, x3, x4pre)
    emit_ln_phase(p, b, x4pre, y_dram=y)
    emit_epilogue(p)
    p.close()
    return p


def stage_fused(inputs):
    x0 = np.asarray(inputs["L0_attn_x"], np.float32)
    shared = dict(
        wq0=np.ascontiguousarray(np.asarray(inputs["L0_attn_q_project"], np.float32).T),
        wk0=np.ascontiguousarray(np.asarray(inputs["L0_attn_k_project"], np.float32).T),
        wv0=np.ascontiguousarray(np.asarray(inputs["L0_attn_v_project"], np.float32).T),
        wo0=np.ascontiguousarray(np.asarray(inputs["L0_attn_o_project"], np.float32).T),
        bqc0=col_layout(np.asarray(inputs["L0_attn_q_bias"], np.float32)),
        bkc0=col_layout(np.asarray(inputs["L0_attn_k_bias"], np.float32)),
        bvb0=bcast_rows(np.asarray(inputs["L0_attn_v_bias"], np.float32)),
        boc0=np.zeros((128, 16), np.float32),
        lnwc=col_layout(np.asarray(inputs["L0_attn_ln_weight"], np.float32)),
        gate_t=np.ascontiguousarray(np.asarray(inputs["L0_ffn_gate"], np.float32).T),
        ebias=bcast_rows(np.asarray(inputs["L0_ffn_expert_bias"], np.float32)),
        w1t=np.ascontiguousarray(np.asarray(inputs["L0_ffn_w1"], np.float32).transpose(0, 2, 1)).reshape(8 * D, 1024),
        w2t=np.ascontiguousarray(np.asarray(inputs["L0_ffn_w2"], np.float32).transpose(0, 2, 1)).reshape(8 * 1024, D),
        w3t=np.ascontiguousarray(np.asarray(inputs["L0_ffn_w3"], np.float32).transpose(0, 2, 1)).reshape(8 * D, 1024),
        sw1t=np.ascontiguousarray(np.asarray(inputs["L0_ffn_shared_w1"], np.float32).T),
        sw2t=np.ascontiguousarray(np.asarray(inputs["L0_ffn_shared_w2"], np.float32).T),
        sw3t=np.ascontiguousarray(np.asarray(inputs["L0_ffn_shared_w3"], np.float32).T),
        esel=esel_matrix(),
        wq1=np.ascontiguousarray(np.asarray(inputs["L1_attn_Wq"], np.float32)),
        wk1=np.ascontiguousarray(np.asarray(inputs["L1_attn_Wk"], np.float32)),
        wv1=np.ascontiguousarray(np.asarray(inputs["L1_attn_Wv"], np.float32)),
        wo1=np.ascontiguousarray(np.asarray(inputs["L1_attn_Wo"], np.float32)),
        bqc1=col_layout(np.asarray(inputs["L1_attn_bq"], np.float32)),
        bkc1=col_layout(np.asarray(inputs["L1_attn_bk"], np.float32)),
        bvb1=bcast_rows(np.asarray(inputs["L1_attn_bv"], np.float32)),
        boc1=col_layout(np.asarray(inputs["L1_attn_bo"], np.float32)),
        wgt=np.ascontiguousarray(np.asarray(inputs["L1_ffn_w_gate"], np.float32).T),
        wut=np.ascontiguousarray(np.asarray(inputs["L1_ffn_w_up"], np.float32).T),
        wdt=np.ascontiguousarray(np.asarray(inputs["L1_ffn_w_down"], np.float32).T),
        pmb=perm_matrix(),
    )
    maps = []
    for c in range(8):
        par, bat = c // 4, c % 4
        pm = pos_map(par)
        cq, sq = rope_tables(pm)
        m = dict(shared)
        m.update(x_own=_fm(x0[bat][pm]), cosq=cq, sinq=sq,
                 maskc=causal_mask(par))
        maps.append(m)
    return maps


def kernel(**inputs):
    p = build_fused()
    res = run_bass_kernel_spmd(p.nc, stage_fused(inputs), core_ids=list(range(8))).results
    out = np.empty((NB, S, D), np.float32)
    for c in range(8):
        par, bat = c // 4, c % 4
        out[bat, pos_map(par)] = np.asarray(res[c]["y"], np.float32)
    return out


def kernel_unfused(**inputs):
    cores = list(range(8))
    pA = build_launchA()
    resA = run_bass_kernel_spmd(pA.nc, stage_A(inputs), core_ids=cores).results
    pB = build_launchB()
    resB = run_bass_kernel_spmd(pB.nc, stage_B(inputs, resA), core_ids=cores).results
    pC = build_launchC()
    resC = run_bass_kernel_spmd(pC.nc, stage_C(inputs, resB), core_ids=cores).results
    pD = build_launchD()
    resD = run_bass_kernel_spmd(pD.nc, stage_D(inputs, resC), core_ids=cores).results
    out = np.empty((NB, S, D), np.float32)
    for c in range(8):
        par, bat = c // 4, c % 4
        out[bat, pos_map(par)] = np.asarray(resD[c]["y"], np.float32)
    return out
